# version 05 (final)
# baseline (speedup 1.0000x reference)
import os
import contextlib
import numpy as np
import ml_dtypes

import concourse.bass as bass
import concourse.mybir as mybir
from concourse.bass_utils import run_bass_kernel_spmd
from concourse.aot_env import allow_ici_all_to_all

F32 = mybir.dt.float32
F16 = mybir.dt.float16
BF16 = mybir.dt.bfloat16
AF = mybir.ActivationFunctionType
ALU = mybir.AluOpType
AXX = mybir.AxisListType.X

NCORES = 8
S, D, T, TH = 8192, 2048, 1024, 512
H = 16
QA, CKV = 1536, 512
NOPE, RO, VD = 128, 64, 128
QHD = NOPE + RO
KVA = CKV + RO
KVB = H * (NOPE + VD)
QB = H * QHD
SCALE0 = float(QHD) ** -0.5
FF = 8192
RND = 1152.0
NEL = float(D * FF)

STAGE = int(os.environ.get("KMK_STAGE", "99"))

WSHAPES = {
    "qaT": (D, QA, D), "qbT": (D, QB, QA), "kvaT": (D, KVA, D),
    "kvbT": (1024, KVB, CKV), "o0T": (D, D, D),
    "gateT": (D, FF, D), "valT": (D, FF, D), "outT": (FF, D, FF),
    "wqT": (D, D, D), "wkT": (D, D, D), "wvT": (D, D, D), "wo1T": (D, D, D),
    "wgT": (D, FF, D), "wuT": (D, FF, D), "wdT": (FF, D, FF),
}
TERNARY = ["gateT", "valT", "outT", "wgT", "wuT", "wdT"]
NONTERN = ["qaT", "kvaT", "qbT", "kvbT", "o0T", "wqT", "wkT", "wvT", "wo1T"]
PAIRSWAP = [p ^ 1 for p in range(32)]


class Sem:
    def __init__(self, stack, nc, name):
        self.h = stack.enter_context(nc.semaphore(name))
        self.v = 0

    def inc(self, inst, n=1):
        inst.then_inc(self.h, n)
        self.v += n
        return inst

    def dma(self, inst):
        return self.inc(inst, 16)

    def wait(self, eng, v=None):
        eng.wait_ge(self.h, self.v if v is None else v)


def build_program():
    nc = bass.Bass("TRN2", target_bir_lowering=False,
                   detect_race_conditions=False)

    def din(name, shape, dtype=F32):
        return nc.dram_tensor(name, shape, dtype, kind="ExternalInput").ap()

    def dint(name, shape, dtype=BF16):
        return nc.dram_tensor(name, shape, dtype).ap()

    xT = din("xT", [D, T])
    W = {k: din(k, [v[0] // 8, v[1]]) for k, v in WSHAPES.items()}
    qalnw = din("qalnw", [128, QA // 128])
    kvalnw = din("kvalnw", [128, CKV // 128])
    gate_bi = din("gate_bi", [128, FF // 128])
    val_bi = din("val_bi", [128, FF // 128])
    out_bi = din("out_bi", [128, D // 128])
    ident_f = din("ident_f", [128, 128])
    ident_b = din("ident_b", [128, 128], BF16)
    ut_b = din("ut_b", [128, 128], BF16)
    cosd0 = din("cosd0", [128, T])
    sins0 = din("sins0", [128, T])
    cosd1 = din("cosd1", [128, T])
    sins1 = din("sins1", [128, T])

    out = nc.dram_tensor("out", [D, T], F32, kind="ExternalOutput").ap()
    dbg = nc.dram_tensor("dbg", [4096, T], F32, kind="ExternalOutput").ap()

    wsh = {k: dint(f"sh_{k}", [v[0] // 8, v[1]]) for k, v in WSHAPES.items()}
    wbf = {k: dint(f"bf_{k}", [v[0], v[1]]) for k, v in WSHAPES.items()}
    gsum_i = dint("gsum_i", [1, 8], F32)
    gsum_o = dint("gsum_o", [1, 8], F32)
    rowb = dint("rowb", [128, 8], F32)

    q0s = dint("q0s", [H, 194, T]); q0g = dint("q0g", [H, 194, T])
    k0s = dint("k0s", [H, 194, T]); k0g = dint("k0g", [H, 194, T])
    v0s = dint("v0s", [H, T, 132]); v0g = dint("v0g", [H, T, 132])
    o0s = dint("o0s", [H, T, 128]); o0g = dint("o0g", [H, T, 128])
    q1s = dint("q1s", [H, 128, T]); q1g = dint("q1g", [H, 128, T])
    k1s = dint("k1s", [H, 128, T]); k1g = dint("k1g", [H, 128, T])
    v1s = dint("v1s", [H, T, 132]); v1g = dint("v1g", [H, T, 132])
    o1s = dint("o1s", [H, T, 128]); o1g = dint("o1g", [H, T, 128])
    gvb = dint("gvb", [FF // 128, 128, T])

    with contextlib.ExitStack() as stack:
        E = stack.enter_context
        b = E(nc.Block())
        s_ld = Sem(stack, nc, "s_ld")
        s_w = Sem(stack, nc, "s_w")
        s_g = Sem(stack, nc, "s_g")
        s_cc = Sem(stack, nc, "s_cc")
        s_pe = Sem(stack, nc, "s_pe")
        s_v = Sem(stack, nc, "s_v")
        s_a = Sem(stack, nc, "s_a")
        s_st = Sem(stack, nc, "s_st")
        s_f = Sem(stack, nc, "s_f")
        s_x = Sem(stack, nc, "s_x")

        uid = [0]

        def sb(name, shape, dtype, st=None):
            uid[0] += 1
            return (st or stack).enter_context(
                nc.sbuf_tensor(f"{name}{uid[0]}", shape, dtype))

        def psum(name, shape, dtype=F32, st=None):
            uid[0] += 1
            return (st or stack).enter_context(
                nc.psum_tensor(f"{name}{uid[0]}", shape, dtype))

        sb_idf = sb("idf", [128, 128], F32)
        sb_idb = sb("idb", [128, 128], BF16)
        sb_ut = sb("ut", [128, 128], BF16)
        sb_ones = sb("ones", [128, 128], F32)
        sb_onesb = sb("onesb", [128, 1], BF16)
        sb_qalnw = sb("qalnw", [128, QA // 128], F32)
        sb_kvalnw = sb("kvalnw", [128, CKV // 128], F32)
        sb_gateb = sb("gateb", [128, FF // 128], F32)
        sb_valb = sb("valb", [128, FF // 128], F32)
        sb_outb = sb("outb", [128, D // 128], F32)
        sb_gw = sb("gw", [128, 8], F32)
        sb_thr = sb("thr", [128, 8], F32)
        sb_nthr = sb("nthr", [128, 8], F32)
        sb_row = sb("rowA", [1, T], F32)
        sb_row2 = sb("rowB", [1, T], F32)
        sb_row3 = sb("rowC", [1, T], F32)
        sb_bc1 = sb("bc1", [128, T], F32)
        sb_bc2 = sb("bc2", [128, T], F32)
        sel_lo = sb("sel_lo", [128, 1], BF16)
        sel_hi = sb("sel_hi", [128, 1], BF16)

        @b.sync
        def _(s):
            for dst, src in [(sb_idf, ident_f), (sb_idb, ident_b), (sb_ut, ut_b),
                             (sb_qalnw, qalnw),
                             (sb_kvalnw, kvalnw), (sb_gateb, gate_bi),
                             (sb_valb, val_bi), (sb_outb, out_bi)]:
                s_ld.dma(s.dma_start(out=dst[:], in_=src[:, :]))
        LD_CONST = s_ld.v

        @b.vector
        def _(v):
            v.memset(sb_ones[:], 1.0)
            v.memset(sel_lo[0:64, :], 1.0)
            v.memset(sel_lo[64:128, :], 0.0)
            v.memset(sel_hi[0:64, :], 0.0)
            v.memset(sel_hi[64:128, :], 1.0)
            s_v.inc(v.memset(sb_onesb[:], 1.0))
        V_ONES = s_v.v

        @b.gpsimd
        def _(g):
            for k in NONTERN:
                s_g.dma(g.dma_start(out=wsh[k][:, :], in_=W[k][:, :]))
            g.wait_ge(s_g.h, s_g.v)
            for k in NONTERN:
                s_cc.inc(g.collective_compute(
                    "AllGather", ALU.bypass, replica_groups=[list(range(NCORES))],
                    ins=[wsh[k].opt()], outs=[wbf[k].opt()]))
        CC_NONTERN = s_cc.v

        prep_plan = []
        for m, k in enumerate(TERNARY):
            for i in range(WSHAPES[k][0] // 8 // 128):
                prep_plan.append((m, k, i))

        with contextlib.ExitStack() as ph:
            wt0 = sb("wt0", [128, FF], F32, ph)
            wt1 = sb("wt1", [128, FF], F32, ph)
            wq0 = sb("wq0", [128, FF], BF16, ph)
            wq1 = sb("wq1", [128, FF], BF16, ph)
            wle = sb("wle", [128, FF], BF16, ph)
            wred = sb("wred", [128, 1], F32, ph)
            wcols = sb("wcols", [128, 8], F32, ph)
            wrow = sb("wrow", [1, 8], F32, ph)
            grow = sb("grow", [1, 8], F32, ph)
            ps_r = psum("psr", [1, 8], F32, ph)
            ps_b8 = psum("psb8", [128, 8], F32, ph)

            LD0 = s_ld.v
            V0 = s_v.v

            @b.sync
            def _(s):
                for idx, (m, k, i) in enumerate(prep_plan):
                    if idx >= 2:
                        s.wait_ge(s_v.h, V0 + 2 * (idx - 1))
                    buf = wt0 if idx % 2 == 0 else wt1
                    c = WSHAPES[k][1]
                    s_ld.dma(s.dma_start(out=buf[:, 0:c],
                                         in_=W[k][i * 128:(i + 1) * 128, 0:c]))

            @b.vector
            def _(v):
                v.memset(wcols[:], 0.0)
                for idx, (m, k, i) in enumerate(prep_plan):
                    v.wait_ge(s_ld.h, LD0 + 16 * (idx + 1))
                    buf = wt0 if idx % 2 == 0 else wt1
                    c = WSHAPES[k][1]
                    rd = v.tensor_reduce(wred[:], buf[:, 0:c], AXX, ALU.add,
                                         apply_absolute_value=True)
                    s_v.inc(rd)
                    v.wait_ge(s_v.h, s_v.v)
                    if i == 0:
                        inst = v.tensor_copy(wcols[:, m:m + 1], wred[:])
                    else:
                        inst = v.tensor_tensor(wcols[:, m:m + 1],
                                               wcols[:, m:m + 1], wred[:], ALU.add)
                    s_v.inc(inst)
            V_SUMS = s_v.v

            @b.tensor
            def _(t):
                t.wait_ge(s_v.h, V_SUMS)
                t.wait_ge(s_v.h, V_ONES)
                s_pe.inc(t.matmul(ps_r[:], sb_ones[:, 0:1], wcols[:],
                                  start=True, stop=True))
            PE_GS = s_pe.v

            @b.vector
            def _(v):
                v.wait_ge(s_pe.h, PE_GS)
                s_v.inc(v.tensor_copy(wrow[:], ps_r[:]))
            V_GROW = s_v.v

            @b.gpsimd
            def _(g):
                g.wait_ge(s_v.h, V_GROW)
                s_g.dma(g.dma_start(out=gsum_i[:, :], in_=wrow[:]))
                g.wait_ge(s_g.h, s_g.v)
                s_cc.inc(g.collective_compute(
                    "AllReduce", ALU.add, replica_groups=[list(range(NCORES))],
                    ins=[gsum_i.opt()], outs=[gsum_o.opt()]))
                g.wait_ge(s_cc.h, s_cc.v)
                s_g.dma(g.dma_start(out=grow[:], in_=gsum_o[:, :]))
            G_GROW = s_g.v

            @b.vector
            def _(v):
                v.wait_ge(s_g.h, G_GROW)
                s_v.inc(v.tensor_scalar_mul(grow[:], grow[:], 1.0 / NEL))
                v.wait_ge(s_v.h, s_v.v)
                s_v.inc(v.tensor_scalar_max(grow[:, 0:3], grow[:, 0:3], 1e-5))
                v.wait_ge(s_v.h, s_v.v)
                s_v.inc(v.tensor_scalar_add(grow[:, 3:6], grow[:, 3:6], 1e-5))
            V_GAMMA = s_v.v

            @b.tensor
            def _(t):
                t.wait_ge(s_v.h, V_GAMMA)
                s_pe.inc(t.matmul(ps_b8[:], sb_ones[0:1, :], grow[:],
                                  start=True, stop=True))
            PE_B8 = s_pe.v

            @b.vector
            def _(v):
                v.wait_ge(s_pe.h, PE_B8)
                s_v.inc(v.tensor_copy(sb_gw[:], ps_b8[:]))
                v.wait_ge(s_v.h, s_v.v)
                s_v.inc(v.tensor_scalar_mul(sb_thr[:], sb_gw[:], 0.5))
                v.wait_ge(s_v.h, s_v.v)
                s_v.inc(v.tensor_scalar_mul(sb_nthr[:], sb_gw[:], -0.5))
            V_THR = s_v.v

            LD1 = s_ld.v
            V1 = s_v.v
            ST0 = s_st.v

            @b.sync
            def _(s):
                for idx, (m, k, i) in enumerate(prep_plan):
                    if idx >= 2:
                        s.wait_ge(s_v.h, V1 + 2 * (idx - 1))
                    buf = wt0 if idx % 2 == 0 else wt1
                    c = WSHAPES[k][1]
                    s_ld.dma(s.dma_start(out=buf[:, 0:c],
                                         in_=W[k][i * 128:(i + 1) * 128, 0:c]))
                    if idx >= 1:
                        mp, kp, ip = prep_plan[idx - 1]
                        s.wait_ge(s_v.h, V1 + 2 * idx)
                        qbuf = wq0 if (idx - 1) % 2 == 0 else wq1
                        cp = WSHAPES[kp][1]
                        s_st.dma(s.dma_start(
                            out=wsh[kp][ip * 128:(ip + 1) * 128, 0:cp],
                            in_=qbuf[:, 0:cp]))
                mp, kp, ip = prep_plan[-1]
                s.wait_ge(s_v.h, V1 + 2 * len(prep_plan))
                qbuf = wq0 if (len(prep_plan) - 1) % 2 == 0 else wq1
                cp = WSHAPES[kp][1]
                s_st.dma(s.dma_start(
                    out=wsh[kp][ip * 128:(ip + 1) * 128, 0:cp],
                    in_=qbuf[:, 0:cp]))

            @b.vector
            def _(v):
                v.wait_ge(s_v.h, V_THR)
                for idx, (m, k, i) in enumerate(prep_plan):
                    v.wait_ge(s_ld.h, LD1 + 16 * (idx + 1))
                    if idx >= 2:
                        v.wait_ge(s_st.h, ST0 + 16 * (idx - 1))
                    buf = wt0 if idx % 2 == 0 else wt1
                    qbuf = wq0 if idx % 2 == 0 else wq1
                    c = WSHAPES[k][1]
                    v.tensor_single_scalar(wle[:, 0:c], buf[:, 0:c],
                                           sb_nthr[:, m:m + 1], ALU.is_le)
                    inst = v.scalar_tensor_tensor(qbuf[:, 0:c], buf[:, 0:c],
                                                  sb_thr[:, m:m + 1],
                                                  wle[:, 0:c],
                                                  ALU.is_ge, ALU.subtract)
                    s_v.inc(inst, 2)
            ST_TERN = s_st.v

            @b.gpsimd
            def _(g):
                g.wait_ge(s_st.h, ST_TERN)
                for k in TERNARY:
                    s_cc.inc(g.collective_compute(
                        "AllGather", ALU.bypass,
                        replica_groups=[list(range(NCORES))],
                        ins=[wsh[k].opt()], outs=[wbf[k].opt()]))
        CC_ALLW = s_cc.v

        CC_AT = {}
        for i, k in enumerate(NONTERN):
            CC_AT[k] = CC_NONTERN - len(NONTERN) + i + 1
        CC_AT["gamma_ar"] = CC_NONTERN + 1
        for i, k in enumerate(TERNARY):
            CC_AT[k] = CC_ALLW - len(TERNARY) + i + 1

        xbf_scope = stack.enter_context(contextlib.ExitStack())
        sb_xbf = sb("xbf", [128, 16 * T], BF16, xbf_scope)
        sb_c0 = sb("c0", [128, T], F32, xbf_scope)
        sb_s0 = sb("s0", [128, T], F32, xbf_scope)

        @b.sync
        def _(s):
            s_ld.dma(s.dma_start(out=sb_c0[:], in_=cosd0[:, :]))
            s_ld.dma(s.dma_start(out=sb_s0[:], in_=sins0[:, :]))
        LD_C0 = s_ld.v

        @b.gpsimd
        def _(g):
            s_g.dma(g.dma_start(
                out=sb_xbf.rearrange("p (c t) -> p c t", c=16),
                in_=xT.rearrange("(c p) t -> p c t", p=128)))
        G_XBF = s_g.v

        def emit_proj(wkey, rhs, nic, out_dim, evac, st, rhs_wait=None,
                      cc_key=None, Fo=512, engine="v", npsum=2):
            rows, cols, real_in = WSHAPES[wkey]
            pr = rows // 8
            n_ranks = (nic * 128 + pr - 1) // pr
            nblocks = (out_dim + Fo - 1) // Fo
            wb0 = sb(f"w_{wkey}a", [128, nic * Fo], BF16, st)
            wb1 = sb(f"w_{wkey}b", [128, nic * Fo], BF16, st)
            pss = [psum(f"pp_{wkey}{i}", [128, TH], F32, st)
                   for i in range(npsum)]
            sem = s_v if engine == "v" else s_a
            W0 = s_w.v
            PE0 = s_pe.v
            E0 = sem.v
            tickets = []
            cpr = pr // 128

            GV, GA, GP, GS = s_v.v, s_a.v, s_pe.v, s_st.v
            def blk_tk(j):
                fo = min(Fo, out_dim - j * Fo)
                return 2 * ((fo + 127) // 128)

            @b.sync
            def _(s):
                s.wait_ge(s_v.h, GV)
                s.wait_ge(s_a.h, GA)
                s.wait_ge(s_pe.h, GP)
                s.wait_ge(s_st.h, GS)
                if cc_key:
                    s.wait_ge(s_cc.h, CC_AT[cc_key])
                for bi in range(nblocks):
                    fo = min(Fo, out_dim - bi * Fo)
                    if bi >= 2:
                        s.wait_ge(s_pe.h, PE0 + sum(
                            blk_tk(j) for j in range(bi - 1)))
                    buf = wb0 if bi % 2 == 0 else wb1
                    for r in range(n_ranks):
                        nch = min(cpr, nic - r * cpr)
                        s_w.dma(s.dma_start(
                            out=buf.rearrange("p (c f) -> p c f", f=Fo)
                                [:, r * cpr:r * cpr + nch, 0:fo],
                            in_=wbf[wkey].rearrange("(rr c p) f -> rr p c f",
                                                    p=128, c=cpr)
                                [r, :, 0:nch, bi * Fo:bi * Fo + fo]))

            @b.tensor
            def _(t):
                if rhs_wait is not None:
                    rhs_wait(t)
                tk = 0
                for bi in range(nblocks):
                    fo = min(Fo, out_dim - bi * Fo)
                    t.wait_ge(s_w.h, W0 + 16 * n_ranks * (bi + 1))
                    for ol in range((fo + 127) // 128):
                        w_oc = min(128, fo - ol * 128)
                        oc = bi * (Fo // 128) + ol
                        for th in range(2):
                            p = pss[tk % npsum]
                            if tk >= npsum:
                                t.wait_ge(sem.h, E0 + (tk - npsum + 1))
                            for ic in range(nic):
                                mm = t.matmul(p[0:w_oc, :],
                                         (wb0 if bi % 2 == 0 else wb1)
                                         [:, ic * Fo + ol * 128:ic * Fo + ol * 128 + w_oc],
                                         rhs[:, ic * T + th * TH:ic * T + (th + 1) * TH],
                                         start=(ic == 0), stop=(ic == nic - 1))
                            s_pe.inc(mm)
                            tickets.append((oc, th, p, s_pe.v, w_oc))
                            tk += 1

            def _ev(eng):
                for i, (oc, th, p, pev, w_oc) in enumerate(tickets):
                    eng.wait_ge(s_pe.h, pev)
                    sem.inc(evac(eng, i, oc, th, p, w_oc))

            if engine == "v":
                @b.vector
                def _(v):
                    _ev(v)
            else:
                @b.scalar
                def _(sc):
                    _ev(sc)

            return tickets

        ps_sh0 = psum("ps_sh0", [128, TH], F32)
        ps_sh1 = psum("ps_sh1", [128, TH], F32)

        def emit_bcast(row_ap, dst):
            @b.tensor
            def _(t):
                t.wait_ge(s_v.h, s_v.v)
                t.matmul(ps_sh0[:], sb_ones[0:1, :], row_ap[0:1, 0:TH],
                         start=True, stop=True)
                mm = t.matmul(ps_sh1[:], sb_ones[0:1, :], row_ap[0:1, TH:T],
                         start=True, stop=True)
                s_pe.inc(mm)
            PEV = s_pe.v

            @b.vector
            def _(v):
                v.wait_ge(s_pe.h, PEV)
                v.tensor_copy(dst[:, 0:TH], ps_sh0[:])
                s_v.inc(v.tensor_copy(dst[:, TH:T], ps_sh1[:]))

        def emit_stat(rhs, nic, dstrow, scale=None, chunks=None):
            use = chunks if chunks is not None else list(range(nic))

            @b.tensor
            def _(t):
                t.wait_ge(s_v.h, s_v.v)
                for th, p in ((0, ps_sh0), (1, ps_sh1)):
                    for j, ic in enumerate(use):
                        mm = t.matmul(p[0:1, 0:TH], sb_onesb[:, 0:1],
                                 rhs[:, ic * T + th * TH:ic * T + (th + 1) * TH],
                                 start=(j == 0), stop=(j == len(use) - 1))
                s_pe.inc(mm)
            PEV = s_pe.v

            @b.vector
            def _(v):
                v.wait_ge(s_pe.h, PEV)
                if scale is None:
                    v.tensor_copy(dstrow[0:1, 0:TH], ps_sh0[0:1, :])
                    s_v.inc(v.tensor_copy(dstrow[0:1, TH:T], ps_sh1[0:1, :]))
                else:
                    v.tensor_scalar_mul(dstrow[0:1, 0:TH], ps_sh0[0:1, :], scale)
                    s_v.inc(v.tensor_scalar_mul(dstrow[0:1, TH:T], ps_sh1[0:1, :], scale))

        def emit_finish(ysrc=None):
            @b.sync
            def _(s):
                s.wait_ge(s_v.h, s_v.v)
                s.wait_ge(s_a.h, s_a.v)
                s.wait_ge(s_pe.h, s_pe.v)
                s.wait_ge(s_g.h, s_g.v)
                s.wait_ge(s_cc.h, s_cc.v)
                if ysrc is not None:
                    s_st.dma(s.dma_start(
                        out=out.rearrange("(c p) t -> p c t", p=128),
                        in_=ysrc.rearrange("p (c t) -> p c t", c=16)))
                s.wait_ge(s_st.h, s_st.v)

        def emit_dump(src_sb, nch, r0=0):
            @b.gpsimd
            def _(g):
                g.wait_ge(s_v.h, s_v.v)
                g.wait_ge(s_a.h, s_a.v)
                g.wait_ge(s_pe.h, s_pe.v)
                g.wait_ge(s_st.h, s_st.v)
                s_g.dma(g.dma_start(
                    out=dbg[r0:r0 + 128 * nch, :].rearrange(
                        "(c p) t -> p c t", p=128),
                    in_=src_sb.rearrange("p (c t) -> p c t", c=nch)))
                g.wait_ge(s_g.h, s_g.v)

        with contextlib.ExitStack() as ph:
            qan = sb("qan", [128, 12 * T], BF16, ph)

            with contextlib.ExitStack() as s1:
                qa_f = sb("qa_f", [128, 12 * T], F32, s1)

                emit_proj("qaT", sb_xbf, 16, QA, lambda e, i, oc, th, p, w: (
                    e.tensor_copy(qa_f[:, oc * T + th * TH:oc * T + (th + 1) * TH], p[:])),
                    s1, rhs_wait=lambda t: t.wait_ge(s_g.h, G_XBF), cc_key="qaT")

                with contextlib.ExitStack() as s2:
                    qa2 = sb("qa2", [128, 12 * T], BF16, s2)

                    @b.vector
                    def _(v):
                        for ic in range(12):
                            s_v.inc(v.tensor_tensor(
                                qa2[:, ic * T:(ic + 1) * T],
                                qa_f[:, ic * T:(ic + 1) * T],
                                qa_f[:, ic * T:(ic + 1) * T], ALU.mult))
                    emit_stat(qa2, 12, sb_row, scale=1.0 / QA)

                @b.vector
                def _(v):
                    s_v.inc(v.tensor_scalar_add(sb_row[:], sb_row[:], 1e-6))

                @b.scalar
                def _(sc):
                    sc.wait_ge(s_v.h, s_v.v)
                    s_a.inc(sc.activation(sb_row2[:], sb_row[:], AF.Sqrt))

                @b.vector
                def _(v):
                    v.wait_ge(s_a.h, s_a.v)
                    s_v.inc(v.reciprocal(sb_row[:], sb_row2[:]))
                emit_bcast(sb_row, sb_bc1)

                @b.vector
                def _(v):
                    v.wait_ge(s_v.h, s_v.v)
                    for ic in range(12):
                        v.tensor_tensor(qa_f[:, ic * T:(ic + 1) * T],
                                        qa_f[:, ic * T:(ic + 1) * T],
                                        sb_bc1[:], ALU.mult)
                        s_v.inc(v.tensor_single_scalar(
                            qan[:, ic * T:(ic + 1) * T],
                            qa_f[:, ic * T:(ic + 1) * T],
                            sb_qalnw[:, ic:ic + 1], ALU.mult))
                V_QAN = s_v.v

            if STAGE == 1:
                emit_dump(qan, 12)
                emit_finish()
                return nc

            sb_qq = sb("qq", [128, 24 * T], BF16, ph)

            def evac_qb(e, i, oc, th, p, w):
                sl = slice(oc * T + th * TH, oc * T + (th + 1) * TH)
                return e.activation(sb_qq[:, sl], p[:], AF.Copy, scale=SCALE0)

            with contextlib.ExitStack() as s3:
                emit_proj("qbT", qan, 12, QB, evac_qb, s3, cc_key="qbT",
                          engine="sc")
            A_QQ = s_a.v

            if STAGE == 2:
                emit_dump(sb_qq, 24)
                emit_finish()
                return nc

            rop_t = sb("ropt", [128, T], F32, ph)
            rop_u = sb("ropu", [128, T], F32, ph)
            rop_s = sb("rops", [128, T], BF16, ph)

            @b.vector
            def _(v):
                v.wait_ge(s_a.h, A_QQ)
                v.wait_ge(s_ld.h, LD_C0)
                for h in range(H):
                    ch = 3 * (h // 2) + 1 + (h % 2)
                    r0 = 0 if h % 2 == 0 else 64
                    qpe = sb_qq[r0:r0 + 64, ch * T:(ch + 1) * T]
                    v.stream_shuffle(rop_s[:, 0:T],
                                     sb_qq[:, ch * T:(ch + 1) * T], PAIRSWAP)
                    v.tensor_tensor(rop_t[r0:r0 + 64, :], qpe,
                                    sb_c0[r0:r0 + 64, :], ALU.mult)
                    v.tensor_tensor(rop_u[r0:r0 + 64, :],
                                    rop_s[r0:r0 + 64, 0:T],
                                    sb_s0[r0:r0 + 64, :], ALU.mult)
                    s_v.inc(v.tensor_tensor(qpe, rop_t[r0:r0 + 64, :],
                                            rop_u[r0:r0 + 64, :], ALU.add))
            V_QROPE = s_v.v

            if STAGE == 21:
                emit_finish()
                return nc

            qsqA = sb("qsqA", [128, T], BF16, ph)
            qsqB = sb("qsqB", [128, T], BF16, ph)
            qnrow = sb("qnrow", [1, T], BF16, ph)
            zrow = sb("zrow", [1, T], BF16, ph)

            @b.vector
            def _(v):
                s_v.inc(v.memset(zrow[:], 0.0))
            V_ZR = s_v.v
            V_PRE_QN = s_v.v
            PE_QN0 = s_pe.v
            G_QN0 = s_g.v

            @b.vector
            def _(v):
                for h in range(H):
                    cA = 3 * (h // 2) + (0 if h % 2 == 0 else 1)
                    cB = cA + 1
                    v.tensor_tensor(qsqA[:], sb_qq[:, cA * T:(cA + 1) * T],
                                    sb_qq[:, cA * T:(cA + 1) * T], ALU.mult)
                    s_v.inc(v.tensor_tensor(
                        qsqB[:], sb_qq[:, cB * T:(cB + 1) * T],
                        sb_qq[:, cB * T:(cB + 1) * T], ALU.mult))
                    v.wait_ge(s_pe.h, PE_QN0 + h + 1)
                    if h > 0:
                        v.wait_ge(s_g.h, G_QN0 + 32 * h)
                    v.tensor_copy(qnrow[0:1, 0:TH], ps_sh0[0:1, :])
                    s_v.inc(v.tensor_copy(qnrow[0:1, TH:T], ps_sh1[0:1, :]))

            @b.tensor
            def _(t):
                for h in range(H):
                    t.wait_ge(s_v.h, V_PRE_QN + 2 * h + 1)
                    selA = sb_onesb if h % 2 == 0 else sel_hi
                    selB = sel_lo if h % 2 == 0 else sb_onesb
                    for th, p in ((0, ps_sh0), (1, ps_sh1)):
                        t.matmul(p[0:1, 0:TH], selA[:, 0:1],
                                 qsqA[:, th * TH:(th + 1) * TH],
                                 start=True, stop=False)
                        mm = t.matmul(p[0:1, 0:TH], selB[:, 0:1],
                                 qsqB[:, th * TH:(th + 1) * TH],
                                 start=False, stop=True)
                    s_pe.inc(mm)

            @b.gpsimd
            def _(g):
                g.wait_ge(s_v.h, V_ZR)
                for h in range(H):
                    g.wait_ge(s_v.h, V_PRE_QN + 2 * h + 2)
                    s_g.dma(g.dma_start(out=q0s[h, 193:194, :],
                                        in_=qnrow[0:1, :]))
                    s_g.dma(g.dma_start(out=q0s[h, 192:193, :],
                                        in_=zrow[0:1, :]))
            V_QNORM = s_v.v

            @b.sync
            def _(s):
                s.wait_ge(s_v.h, V_QNORM)
                for h in range(H):
                    base = h * QHD
                    c0, r0 = divmod(base, 128)
                    if r0 == 0:
                        s_st.dma(s.dma_start(out=q0s[h, 0:128, :],
                                             in_=sb_qq[:, c0 * T:(c0 + 1) * T]))
                    else:
                        s_st.dma(s.dma_start(
                            out=q0s[h, 0:64, :],
                            in_=sb_qq[64:128, c0 * T:(c0 + 1) * T]))
                        s_st.dma(s.dma_start(
                            out=q0s[h, 64:128, :],
                            in_=sb_qq[0:64, (c0 + 1) * T:(c0 + 2) * T]))
                    ch = 3 * (h // 2) + 1 + (h % 2)
                    rp = 0 if h % 2 == 0 else 64
                    s_st.dma(s.dma_start(out=q0s[h, 128:192, :],
                                         in_=sb_qq[rp:rp + 64, ch * T:(ch + 1) * T]))
            ST_QSEND = s_st.v

            if STAGE == 22:
                emit_finish()
                return nc

        with contextlib.ExitStack() as ph:
            kva_f = sb("kva_f", [128, 5 * T], F32, ph)
            kpe_b = sb("kpeb", [128, T], BF16, ph)
            rop_s2 = sb("rops2", [128, T], BF16, ph)
            rop_t2 = sb("ropt2", [128, T], F32, ph)
            rop_u2 = sb("ropu2", [128, T], F32, ph)
            kpe_r = sb("kpe_r", [128, T], BF16, ph)

            def evac_kva(e, i, oc, th, p, w):
                sl = slice(oc * T + th * TH, oc * T + (th + 1) * TH)
                return e.tensor_copy(kva_f[0:w, sl], p[0:w, :])

            with contextlib.ExitStack() as s4:
                emit_proj("kvaT", sb_xbf, 16, KVA, evac_kva, s4,
                          rhs_wait=lambda t: t.wait_ge(s_g.h, G_XBF),
                          cc_key="kvaT")

            if STAGE == 23:
                emit_finish()
                return nc

            ck2 = sb("ck2", [128, 4 * T], BF16, ph)

            @b.vector
            def _(v):
                for ic in range(4):
                    s_v.inc(v.tensor_tensor(ck2[:, ic * T:(ic + 1) * T],
                                            kva_f[:, ic * T:(ic + 1) * T],
                                            kva_f[:, ic * T:(ic + 1) * T],
                                            ALU.mult))
            emit_stat(ck2, 4, sb_row, scale=1.0 / CKV)

            @b.vector
            def _(v):
                s_v.inc(v.tensor_scalar_add(sb_row[:], sb_row[:], 1e-6))

            @b.scalar
            def _(sc):
                sc.wait_ge(s_v.h, s_v.v)
                s_a.inc(sc.activation(sb_row2[:], sb_row[:], AF.Sqrt))

            @b.vector
            def _(v):
                v.wait_ge(s_a.h, s_a.v)
                s_v.inc(v.reciprocal(sb_row[:], sb_row2[:]))
            emit_bcast(sb_row, sb_bc1)

            ckvn = sb("ckvn", [128, 4 * T], BF16, ph)

            @b.vector
            def _(v):
                v.wait_ge(s_v.h, s_v.v)
                for ic in range(4):
                    v.tensor_tensor(kva_f[:, ic * T:(ic + 1) * T],
                                    kva_f[:, ic * T:(ic + 1) * T], sb_bc1[:],
                                    ALU.mult)
                    s_v.inc(v.tensor_single_scalar(
                        ckvn[:, ic * T:(ic + 1) * T],
                        kva_f[:, ic * T:(ic + 1) * T],
                        sb_kvalnw[:, ic:ic + 1], ALU.mult))

                if STAGE != 24:
                    kpe = kva_f[0:64, 4 * T:5 * T]
                    v.tensor_copy(kpe_b[0:64, :], kpe)
                    v.stream_shuffle(rop_s2[0:64, :], kpe_b[0:64, :], PAIRSWAP)
                    v.tensor_tensor(rop_t2[0:64, :], kpe, sb_c0[0:64, :], ALU.mult)
                    v.tensor_tensor(rop_u2[0:64, :], rop_s2[0:64, :],
                                    sb_s0[0:64, :], ALU.mult)
                    s_v.inc(v.tensor_tensor(kpe_r[0:64, :], rop_t2[0:64, :],
                                            rop_u2[0:64, :], ALU.add))
            V_CKVN = s_v.v

            if STAGE == 24:
                emit_finish()
                return nc

            if STAGE == 3:
                emit_dump(ckvn, 4)
                emit_dump(kpe_r, 1, r0=4 * 128)
                emit_finish()
                return nc

            kvst = sb("kvst", [128, 2 * T], BF16, ph)
            ksq = sb("ksq", [128, T], BF16, ph)
            kpesq = sb("kpesq", [128, T], BF16, ph)
            vts = sb("vts", [128, 8 * 132], BF16, ph)
            knr = sb("knr", [1, T], BF16, ph)
            wkv0 = sb("wkv0", [128, 4 * TH], BF16, ph)
            wkv1 = sb("wkv1", [128, 4 * TH], BF16, ph)
            ps_tr = psum("ps_tr", [128, 8 * 128], BF16, ph)
            ps_kv = [psum(f"ps_kv{i}", [128, TH], F32, ph) for i in range(4)]

            zrow2 = sb("zrow2", [1, T], BF16, ph)

            @b.vector
            def _(v):
                v.wait_ge(s_v.h, V_CKVN)
                v.memset(zrow2[:], 0.0)
                for g in range(8):
                    v.memset(vts[:, g * 132 + 128:g * 132 + 129], 1.0)
                v.memset(
                    vts.rearrange("p (g e) -> p g e", g=8)[:, :, 129:132], 0.0)
                v.memset(kpesq[64:128, :], 0.0)
                s_v.inc(v.tensor_tensor(kpesq[0:64, :], kpe_r[0:64, :],
                                        kpe_r[0:64, :], ALU.mult))
            V_VTS = s_v.v
            V_ZR2 = V_VTS

            W_KV0 = s_w.v
            PE_KV0 = s_pe.v
            V_KV0 = s_v.v
            A_KV0 = s_a.v
            ST_KV0 = s_g.v

            GV_KV, GA_KV, GP_KV = s_v.v, s_a.v, s_pe.v

            @b.sync
            def _(s):
                s.wait_ge(s_cc.h, CC_AT["kvbT"])
                s.wait_ge(s_v.h, GV_KV)
                s.wait_ge(s_a.h, GA_KV)
                s.wait_ge(s_pe.h, GP_KV)
                for bi in range(8):
                    if bi >= 2:
                        s.wait_ge(s_pe.h, PE_KV0 + 3 * (2 * bi - 2))
                    buf = wkv0 if bi % 2 == 0 else wkv1
                    for r in range(4):
                        s_w.dma(s.dma_start(
                            out=buf.rearrange("p (c f) -> p c f", f=TH)
                                [:, r:r + 1, :],
                            in_=wbf["kvbT"].rearrange("(rr p) f -> rr p f", p=128)
                                [r, :, bi * TH:(bi + 1) * TH]))

            @b.tensor
            def _(t):
                t.wait_ge(s_v.h, V_VTS)
                for h in range(H):
                    bi = h // 2
                    buf = wkv0 if bi % 2 == 0 else wkv1
                    off = (h % 2) * 2 * 128
                    t.wait_ge(s_w.h, W_KV0 + 16 * 4 * (bi + 1))
                    if h > 0:
                        t.wait_ge(s_v.h, V_KV0 + 7 * h)
                    for th in range(2):
                        for ic in range(4):
                            t.matmul(ps_kv[th][:],
                                     buf[:, ic * TH + off:ic * TH + off + 128],
                                     ckvn[:, ic * T + th * TH:ic * T + (th + 1) * TH],
                                     start=(ic == 0), stop=(ic == 3))
                        for ic in range(4):
                            mm = t.matmul(ps_kv[2 + th][:],
                                     buf[:, ic * TH + off + 128:ic * TH + off + 256],
                                     ckvn[:, ic * T + th * TH:ic * T + (th + 1) * TH],
                                     start=(ic == 0), stop=(ic == 3))
                    s_pe.inc(mm)
                    t.wait_ge(s_v.h, V_KV0 + 7 * h + 4)
                    for g in range(8):
                        mm = t.transpose(ps_tr[:, g * 128:(g + 1) * 128],
                                    kvst[:, T + g * 128:T + (g + 1) * 128],
                                    sb_idb[:])
                    s_pe.inc(mm)
                    t.wait_ge(s_v.h, V_KV0 + 7 * h + 5)
                    for th, p in ((0, ps_sh0), (1, ps_sh1)):
                        t.matmul(p[0:1, 0:TH], sb_onesb[:, 0:1],
                                 ksq[:, th * TH:(th + 1) * TH],
                                 start=True, stop=False)
                        mm = t.matmul(p[0:1, 0:TH], sel_lo[:, 0:1],
                                 kpesq[:, th * TH:(th + 1) * TH],
                                 start=False, stop=True)
                    s_pe.inc(mm)
            PE_KVE = s_pe.v

            @b.vector
            def _(v):
                for h in range(H):
                    v.wait_ge(s_pe.h, PE_KV0 + 3 * h + 1)
                    if h > 0:
                        v.wait_ge(s_g.h, ST_KV0 + 16 * 5 * h)
                    v.tensor_copy(kvst[:, 0:TH], ps_kv[0][:])
                    v.tensor_copy(kvst[:, TH:T], ps_kv[1][:])
                    v.tensor_copy(kvst[:, T:T + TH], ps_kv[2][:])
                    cp = v.tensor_copy(kvst[:, T + TH:2 * T], ps_kv[3][:])
                    s_v.inc(cp, 3)
                    s_v.inc(v.tensor_tensor(ksq[:, :], kvst[:, 0:T],
                                            kvst[:, 0:T], ALU.mult))
                    v.wait_ge(s_pe.h, PE_KV0 + 3 * h + 2)
                    for g in range(8):
                        cp = v.tensor_copy(vts[:, g * 132:g * 132 + 128],
                                      ps_tr[:, g * 128:(g + 1) * 128])
                    s_v.inc(cp)
                    v.wait_ge(s_pe.h, PE_KV0 + 3 * h + 3)
                    v.tensor_copy(knr[0:1, 0:TH], ps_sh0[0:1, :])
                    cp = v.tensor_copy(knr[0:1, TH:T], ps_sh1[0:1, :])
                    s_v.inc(cp, 2)

            @b.gpsimd
            def _(g):
                g.wait_ge(s_v.h, V_ZR2)
                for h in range(H):
                    g.wait_ge(s_v.h, V_KV0 + 7 * h + 7)
                    s_g.dma(g.dma_start(out=k0s[h, 0:128, :],
                                        in_=kvst[:, 0:T]))
                    s_g.dma(g.dma_start(out=k0s[h, 128:192, :],
                                        in_=kpe_r[0:64, :]))
                    s_g.dma(g.dma_start(out=k0s[h, 193:194, :],
                                        in_=knr[0:1, :]))
                    s_g.dma(g.dma_start(out=k0s[h, 192:193, :],
                                        in_=zrow2[0:1, :]))
                    s_g.dma(g.dma_start(
                        out=v0s.rearrange("h (g p) e -> h p g e", p=128)[h],
                        in_=vts.rearrange("p (g e) -> p g e", g=8)))
            ST_KVSEND = s_g.v
        xbf_scope.close()

        @b.gpsimd
        def _(g):
            g.wait_ge(s_st.h, s_st.v)
            g.wait_ge(s_g.h, s_g.v)
            with allow_ici_all_to_all():
                for si, so in ((q0s, q0g), (k0s, k0g), (v0s, v0g)):
                    s_cc.inc(g.collective_compute(
                        "AllToAll", ALU.bypass,
                        replica_groups=[list(range(NCORES))],
                        ins=[si.opt()], outs=[so.opt()]))
        CC_A2A0 = s_cc.v

        def emit_attn(layer, j, qg, kg, vg, osnd, cc_wait):
            with contextlib.ExitStack() as ah:
                ka = sb("ka", [128, 8 * T], BF16, ah)
                qa_ = sb("qa_", [128, 8 * T], BF16, ah)
                vv = sb("vv", [128, 64 * 132], BF16, ah)
                if layer == 0:
                    kb = sb("kb", [65, 8 * T], BF16, ah)
                    qb_ = sb("qb_", [65, 8 * T], BF16, ah)
                    knrm = sb("knrm", [1, 8 * T], BF16, ah)
                    qnrm = sb("qnrm", [1, 8 * T], BF16, ah)
                mx = sb("mx", [1, 1], F32, ah)
                pt = [sb("pt", [128, TH], BF16, ah) for _ in range(2)]
                sq_f = sb("sqf", [128, TH], F32, ah)
                ot = sb("ot", [128, 4 * 128], BF16, ah)
                den = sb("den", [128, 1], F32, ah)
                ps_s = [psum("ps_s", [128, TH], F32, ah) for _ in range(2)]
                pacc = [psum("pacc", [128, 132], F32, ah) for _ in range(4)]

                LD0 = s_ld.v
                GV0, GA0, GP0, GS0 = s_v.v, s_a.v, s_pe.v, s_st.v

                @b.sync
                def _(s):
                    s.wait_ge(s_cc.h, cc_wait)
                    s.wait_ge(s_v.h, GV0)
                    s.wait_ge(s_a.h, GA0)
                    s.wait_ge(s_pe.h, GP0)
                    for r in range(8):
                        s_ld.dma(s.dma_start(out=ka[:, r * T:(r + 1) * T],
                                             in_=kg[2 * r + j, 0:128, :]))
                        s_ld.dma(s.dma_start(out=qa_[:, r * T:(r + 1) * T],
                                             in_=qg[2 * r + j, 0:128, :]))
                        if layer == 0:
                            s_ld.dma(s.dma_start(out=kb[:, r * T:(r + 1) * T],
                                                 in_=kg[2 * r + j, 128:193, :]))
                            s_ld.dma(s.dma_start(out=qb_[:, r * T:(r + 1) * T],
                                                 in_=qg[2 * r + j, 128:193, :]))
                            s_ld.dma(s.dma_start(out=knrm[0:1, r * T:(r + 1) * T],
                                                 in_=kg[2 * r + j, 193:194, :]))
                            s_ld.dma(s.dma_start(out=qnrm[0:1, r * T:(r + 1) * T],
                                                 in_=qg[2 * r + j, 193:194, :]))
                        s_ld.dma(s.dma_start(
                            out=vv.rearrange("p (G e) -> p G e", e=132)
                                [:, 8 * r:8 * r + 8, :],
                            in_=vg[2 * r + j].rearrange("(g p) e -> p g e",
                                                        p=128)))
                LD_ATT = s_ld.v

                V_PRE = s_v.v
                A_PRE = s_a.v
                if layer == 0:
                    @b.vector
                    def _(v):
                        v.wait_ge(s_ld.h, LD_ATT)
                        v.memset(kb[64:65, :], 1.0)
                        rd = v.tensor_reduce(mx[:], knrm[0:1, :], AXX, ALU.max)
                        s_v.inc(rd)

                    @b.scalar
                    def _(sc):
                        sc.wait_ge(s_v.h, s_v.v)
                        s_a.inc(sc.activation(mx[:], mx[:], AF.Sqrt))

                    @b.vector
                    def _(v):
                        v.wait_ge(s_a.h, s_a.v)
                        s_v.inc(v.tensor_scalar_mul(mx[:], mx[:], -1.0))
                    V_MX = s_v.v

                    @b.scalar
                    def _(sc):
                        sc.wait_ge(s_v.h, V_MX)
                        for r in range(8):
                            s_a.inc(sc.activation(sb_row[0:1, :],
                                                  qnrm[0:1, r * T:(r + 1) * T],
                                                  AF.Sqrt))
                            sc.wait_ge(s_v.h, V_MX + r + 1)

                    @b.vector
                    def _(v):
                        for r in range(8):
                            v.wait_ge(s_a.h, A_PRE + 1 + r + 1)
                            s_v.inc(v.tensor_single_scalar(
                                qb_[64:65, r * T:(r + 1) * T], sb_row[0:1, :],
                                mx[0:1, 0:1], ALU.mult))
                PRE_DONE_V = s_v.v
                PRE_DONE_A = s_a.v

                PE0 = s_pe.v
                V0 = s_v.v
                A0 = s_a.v
                ST0 = s_st.v
                njobs = []
                for Q in range(16):
                    for kc in range(4 * Q + 4):
                        njobs.append((Q, kc))
                NJ = len(njobs)
                pe_at_score = {}
                pe_at_attn = {}
                c = PE0
                sched = []
                for i in range(NJ):
                    sched.append(("s", i))
                    c += 1
                    pe_at_score[i] = c
                    if i >= 1:
                        sched.append(("a", i - 1))
                        c += 1
                        pe_at_attn[i - 1] = c
                sched.append(("a", NJ - 1))
                c += 1
                pe_at_attn[NJ - 1] = c
                FIN0 = s_f.v

                @b.tensor
                def _(t):
                    t.wait_ge(s_ld.h, LD_ATT)
                    t.wait_ge(s_v.h, PRE_DONE_V)
                    t.wait_ge(s_a.h, PRE_DONE_A)
                    for op, i in sched:
                        Q, kc = njobs[i]
                        if op == "s":
                            pp = ps_s[i % 2]
                            if i >= 2:
                                t.wait_ge(s_a.h, A0 + (i - 1))
                            mm = t.matmul(pp[:], ka[:, kc * 128:(kc + 1) * 128],
                                     qa_[:, Q * TH:(Q + 1) * TH],
                                     start=True, stop=(layer == 1))
                            if layer == 0:
                                mm = t.matmul(pp[:],
                                         kb[0:65, kc * 128:(kc + 1) * 128],
                                         qb_[0:65, Q * TH:(Q + 1) * TH],
                                         start=False, stop=True)
                            s_pe.inc(mm)
                        else:
                            t.wait_ge(s_v.h, V0 + 2 * (i + 1))
                            if kc == 0 and Q > 0:
                                t.wait_ge(s_f.h, FIN0 + 4 * Q)
                            buf = pt[i % 2]
                            for sj in range(4):
                                mm = t.matmul(pacc[sj][0:128, 0:129],
                                         buf[:, sj * 128:(sj + 1) * 128],
                                         vv[:, kc * 132:kc * 132 + 129],
                                         start=(kc == 0), stop=(kc == 4 * Q + 3))
                            s_pe.inc(mm)

                @b.scalar
                def _(sc):
                    for i in range(NJ):
                        sc.wait_ge(s_pe.h, pe_at_score[i])
                        if i >= 2:
                            sc.wait_ge(s_pe.h, pe_at_attn[i - 2])
                        if layer == 1 and i >= 1:
                            sc.wait_ge(s_v.h, V0 + 2 * i)
                        if layer == 0:
                            s_a.inc(sc.activation(pt[i % 2][:], ps_s[i % 2][:],
                                                  AF.Exp))
                        else:
                            s_a.inc(sc.activation(sq_f[:], ps_s[i % 2][:],
                                                  AF.Square))

                @b.vector
                def _(v):
                    def fence(inst):
                        s_x.inc(inst)
                        v.wait_ge(s_x.h, s_x.v)
                    fin = 0
                    for i, (Q, kc) in enumerate(njobs):
                        v.wait_ge(s_a.h, A0 + i + 1)
                        buf = pt[i % 2]
                        last_v = None
                        dj = kc - 4 * Q
                        if layer == 1:
                            last_v = v.tensor_tensor(buf[:], sq_f[:], sq_f[:], ALU.mult)
                            if dj >= 0:
                                fence(last_v)
                        if dj >= 0:
                            if dj > 0:
                                v.memset(buf[:, 0:dj * 128], 0.0)
                            last_v = v.tensor_tensor(buf[:, dj * 128:(dj + 1) * 128],
                                            buf[:, dj * 128:(dj + 1) * 128],
                                            sb_ut[:], ALU.mult)
                        s_v.inc(last_v if last_v is not None else v.nop(), 2)
                        if kc == 4 * Q + 3:
                            v.wait_ge(s_pe.h, pe_at_attn[i])
                            for sj in range(4):
                                if fin >= 4:
                                    v.wait_ge(s_st.h, ST0 + 16 * (fin - 3))
                                fence(v.tensor_scalar_max(
                                    den[:], pacc[sj][0:128, 128:129],
                                    1e-37 if layer == 0 else 1e-6))
                                fence(v.reciprocal(den[:], den[:]))
                                fo_i = v.tensor_single_scalar(
                                    ot[:, sj * 128:(sj + 1) * 128],
                                    pacc[sj][0:128, 0:128], den[:, 0:1],
                                    ALU.mult)
                                s_f.inc(fo_i)
                                fin += 1

                @b.sync
                def _(s):
                    fin = 0
                    for Q in range(16):
                        for sj in range(4):
                            fin += 1
                            s.wait_ge(s_f.h, FIN0 + fin)
                            s_st.dma(s.dma_start(
                                out=osnd[2 * (Q // 2) + j,
                                         TH * (Q % 2) + sj * 128:
                                         TH * (Q % 2) + (sj + 1) * 128, :],
                                in_=ot[:, sj * 128:(sj + 1) * 128]))
                    s.wait_ge(s_st.h, s_st.v)

            return None

        for j in range(2):
            emit_attn(0, j, q0g, k0g, v0g, o0s, CC_A2A0)

        @b.gpsimd
        def _(g):
            g.wait_ge(s_st.h, s_st.v)
            with allow_ici_all_to_all():
                s_cc.inc(g.collective_compute(
                    "AllToAll", ALU.bypass, replica_groups=[list(range(NCORES))],
                    ins=[o0s.opt()], outs=[o0g.opt()]))
        CC_O0 = s_cc.v

        ybk = dint("ybk", [D, T], F32)
        l0_scope = stack.enter_context(contextlib.ExitStack())
        sb_y = sb("y", [128, 16 * T], F32, l0_scope)

        def emit_ln(sb_y):
            with contextlib.ExitStack() as lh:
                y2 = [sb("y2", [128, TH], F32, lh) for _ in range(2)]
                V0 = s_v.v

                @b.tensor
                def _(t):
                    t.wait_ge(s_v.h, V0)
                    t.wait_ge(s_a.h, s_a.v)
                    for th, p in ((0, ps_sh0), (1, ps_sh1)):
                        for ic in range(16):
                            mm = t.matmul(p[0:1, 0:TH], sb_ones[:, 0:1],
                                     sb_y[:, ic * T + th * TH:ic * T + (th + 1) * TH],
                                     start=(ic == 0), stop=(ic == 15))
                    s_pe.inc(mm)
                PE_SY = s_pe.v

                @b.vector
                def _(v):
                    v.wait_ge(s_pe.h, PE_SY)
                    v.tensor_scalar_mul(sb_row[0:1, 0:TH], ps_sh0[0:1, :],
                                        1.0 / D)
                    s_v.inc(v.tensor_scalar_mul(sb_row[0:1, TH:T],
                                                ps_sh1[0:1, :], 1.0 / D))
                V_MEAN = s_v.v

                @b.vector
                def _(v):
                    for i in range(32):
                        ic, th = divmod(i, 2)
                        if i >= 2:
                            v.wait_ge(s_pe.h, PE_SY + i - 1)
                        sl = slice(ic * T + th * TH, ic * T + (th + 1) * TH)
                        s_v.inc(v.tensor_tensor(y2[i % 2][:], sb_y[:, sl],
                                                sb_y[:, sl], ALU.mult))

                @b.tensor
                def _(t):
                    for i in range(32):
                        ic, th = divmod(i, 2)
                        t.wait_ge(s_v.h, V_MEAN + i + 1)
                        p = ps_sh0 if th == 0 else ps_sh1
                        mm = t.matmul(p[0:1, 0:TH], sb_ones[:, 0:1], y2[i % 2][:],
                                 start=(ic == 0), stop=(ic == 15))
                        s_pe.inc(mm)
                PE_SY2 = s_pe.v

                @b.vector
                def _(v):
                    v.wait_ge(s_pe.h, PE_SY2)
                    v.tensor_scalar_mul(sb_row2[0:1, 0:TH], ps_sh0[0:1, :],
                                        1.0 / D)
                    v.tensor_scalar_mul(sb_row2[0:1, TH:T], ps_sh1[0:1, :],
                                        1.0 / D)
                    v.tensor_tensor(sb_row3[:], sb_row[:], sb_row[:], ALU.mult)
                    v.tensor_tensor(sb_row2[:], sb_row2[:], sb_row3[:],
                                    ALU.subtract)
                    s_v.inc(v.tensor_scalar_add(sb_row2[:], sb_row2[:], 1e-5))

                @b.scalar
                def _(sc):
                    sc.wait_ge(s_v.h, s_v.v)
                    s_a.inc(sc.activation(sb_row3[:], sb_row2[:], AF.Sqrt))

                @b.vector
                def _(v):
                    v.wait_ge(s_a.h, s_a.v)
                    v.reciprocal(sb_row2[:], sb_row3[:])
                    s_v.inc(v.tensor_tensor(sb_row3[:], sb_row[:], sb_row2[:],
                                            ALU.mult))
                emit_bcast(sb_row2, sb_bc1)
                emit_bcast(sb_row3, sb_bc2)

                @b.vector
                def _(v):
                    v.wait_ge(s_v.h, s_v.v)
                    for ic in range(16):
                        sl = slice(ic * T, (ic + 1) * T)
                        v.tensor_tensor(sb_y[:, sl], sb_y[:, sl], sb_bc1[:],
                                        ALU.mult)
                        s_v.inc(v.tensor_tensor(sb_y[:, sl], sb_y[:, sl],
                                                sb_bc2[:], ALU.subtract))

        def emit_oproj_ln(layer, og, wkey, cc_wait, sb_y, load_from):
            with contextlib.ExitStack() as oh:
                afm = sb("afm", [128, 16 * T], BF16, oh)
                otm = sb("otm", [128, 8 * 128], BF16, oh)
                ps_t = psum("ps_t", [128, 8 * 128], BF16, oh)
                LD0 = s_ld.v
                GV0, GA0, GP0 = s_v.v, s_a.v, s_pe.v

                @b.sync
                def _(s):
                    s.wait_ge(s_cc.h, cc_wait)
                    s.wait_ge(s_v.h, GV0)
                    s.wait_ge(s_a.h, GA0)
                    s.wait_ge(s_pe.h, GP0)
                    if load_from is not None:
                        s_ld.dma(s.dma_start(
                            out=sb_y.rearrange("p (c t) -> p c t", c=16),
                            in_=load_from.rearrange("(c p) t -> p c t", p=128)))
                    for h in range(H):
                        if h >= 1:
                            s.wait_ge(s_v.h, GV0 + h)
                        s_ld.dma(s.dma_start(
                            out=otm.rearrange("p (g e) -> p g e", g=8),
                            in_=og[h].rearrange("(g p) e -> p g e", p=128)))

                @b.tensor
                def _(t):
                    for h in range(H):
                        t.wait_ge(s_ld.h, LD0 + 16 * (h + 1) + (16 if load_from is not None else 0))
                        if h >= 1:
                            t.wait_ge(s_v.h, GV0 + h)
                        for g in range(8):
                            mm = t.transpose(ps_t[:, g * 128:(g + 1) * 128],
                                        otm[:, g * 128:(g + 1) * 128], sb_idb[:])
                        s_pe.inc(mm)
                PE_TR0 = s_pe.v - H

                @b.vector
                def _(v):
                    for h in range(H):
                        v.wait_ge(s_pe.h, PE_TR0 + h + 1)
                        for g in range(8):
                            cp = v.tensor_copy(afm[:, h * T + g * 128:h * T + (g + 1) * 128],
                                          ps_t[:, g * 128:(g + 1) * 128])
                        s_v.inc(cp)
                V_AFM = s_v.v

                def evac_o(e, i, oc, th, p, w):
                    sl = slice(oc * T + th * TH, oc * T + (th + 1) * TH)
                    return e.tensor_tensor(sb_y[:, sl], p[:], sb_y[:, sl],
                                           ALU.add)

                xw = LD0 + (16 if load_from is not None else 0)
                emit_proj(wkey, afm, 16, D, evac_o, oh, cc_key=wkey,
                          rhs_wait=lambda t: (t.wait_ge(s_ld.h, xw),
                                              t.wait_ge(s_v.h, V_AFM)))

            emit_ln(sb_y)
            return None

        emit_oproj_ln(0, o0g, "o0T", CC_O0, sb_y, load_from=xT)

        if STAGE == 4:
            emit_dump(sb_y, 16)
            emit_finish()
            return nc

        def emit_rows_from_macc(macc, clip_lo):
            with contextlib.ExitStack() as gh:
                gpp = sb("gpp", [128, 8], F32, gh)
                gp2 = sb("gp2", [128, 8], F32, gh)
                snp = (sb("snp", [128, 48], F32, gh)
                       if STAGE == 43 and not _PROBED[0] else None)
                V0 = s_v.v

                @b.tensor
                def _(t):
                    t.wait_ge(s_v.h, V0)
                    for g in range(8):
                        p = ps_sh0 if g < 4 else ps_sh1
                        mm = t.transpose(p[:, (g % 4) * 128:(g % 4 + 1) * 128],
                                    macc[:, g * 128:(g + 1) * 128], sb_idf[:])
                    s_pe.inc(mm)
                PE_TR = s_pe.v

                @b.vector
                def _(v):
                    def fence(inst):
                        s_v.inc(inst)
                        v.wait_ge(s_v.h, s_v.v)
                    v.wait_ge(s_pe.h, PE_TR)
                    for g in range(8):
                        rd = v.tensor_reduce(
                            gpp[:, g:g + 1],
                            (ps_sh0 if g < 4 else ps_sh1)
                            [:, (g % 4) * 128:(g % 4 + 1) * 128],
                            AXX, ALU.max)
                    fence(rd)
                    if snp is not None:
                        fence(v.tensor_copy(snp[:, 0:8], gpp[:]))
                    if clip_lo:
                        fence(v.tensor_scalar_max(gpp[:], gpp[:], 1e-5))
                        if snp is not None:
                            fence(v.tensor_copy(snp[:, 8:16], gpp[:]))
                        fence(v.reciprocal(gp2[:], gpp[:]))
                        if snp is not None:
                            fence(v.tensor_copy(snp[:, 16:24], gp2[:]))
                        fence(v.tensor_scalar_mul(gp2[:], gp2[:], 127.0))
                        if snp is not None:
                            fence(v.tensor_copy(snp[:, 24:32], gp2[:]))
                        lv = v.reciprocal(gpp[:], gp2[:])
                        if snp is not None:
                            fence(lv)
                            v.tensor_copy(snp[:, 32:40], gpp[:])
                            lv = v.tensor_copy(snp[:, 40:48], gp2[:])
                    else:
                        fence(v.tensor_scalar(gp2[:], gpp[:], 1.0 / 127.0, 1e-5,
                                              ALU.mult, ALU.add))
                        lv = v.reciprocal(gpp[:], gp2[:])
                    s_v.inc(lv)
                V_GPP = s_v.v

                if snp is not None:
                    _PROBED[0] = True

                    @b.gpsimd
                    def _(g):
                        g.wait_ge(s_v.h, V_GPP)
                        s_g.dma(g.dma_start(out=dbg[0:128, 0:48], in_=snp[:]))
                        g.wait_ge(s_g.h, s_g.v)
                    emit_finish()
                    return True

                mult_pp = gp2 if clip_lo else gpp
                osc_pp = gpp if clip_lo else gp2

                @b.sync
                def _(s):
                    s.wait_ge(s_v.h, V_GPP)
                    s_st.dma(s.dma_start(out=rowb[:, 0:8], in_=mult_pp[:]))
                    s.wait_ge(s_st.h, s_st.v)
                    s_ld.dma(s.dma_start(
                        out=sb_row[0:1, :],
                        in_=rowb.rearrange("p g -> g p"),
                        allow_slow_non_contiguous=True))
                    s.wait_ge(s_ld.h, s_ld.v)
                    s_st.dma(s.dma_start(out=rowb[:, 0:8], in_=osc_pp[:]))
                    s.wait_ge(s_st.h, s_st.v)
                    s_ld.dma(s.dma_start(
                        out=sb_row2[0:1, :],
                        in_=rowb.rearrange("p g -> g p"),
                        allow_slow_non_contiguous=True))
                LD_ROWS = s_ld.v

                @b.vector
                def _(v):
                    v.wait_ge(s_ld.h, LD_ROWS)
                    s_v.inc(v.nop())
                emit_bcast(sb_row, sb_bc1)
                emit_bcast(sb_row2, sb_bc2)

                if STAGE == 42 and not _PROBED[0]:
                    _PROBED[0] = True

                    @b.gpsimd
                    def _(g):
                        g.wait_ge(s_v.h, s_v.v)
                        s_g.dma(g.dma_start(out=dbg[0:128, 0:8], in_=gpp[:]))
                        s_g.dma(g.dma_start(out=dbg[128:256, 0:8], in_=gp2[:]))
                        s_g.dma(g.dma_start(out=dbg[256:257, :],
                                            in_=sb_row[0:1, :]))
                        s_g.dma(g.dma_start(out=dbg[384:385, :],
                                            in_=sb_row2[0:1, :]))
                        s_g.dma(g.dma_start(out=dbg[512:640, 0:8],
                                            in_=rowb[:, :]))
                        g.wait_ge(s_g.h, s_g.v)
                    return True

        _PROBED = [False]

        def emit_gamma_rows(src, nch, clip_lo):
            with contextlib.ExitStack() as gh:
                macc = sb("macc", [128, T], F32, gh)
                V0 = s_v.v

                @b.vector
                def _(v):
                    v.wait_ge(s_v.h, V0)
                    v.memset(macc[:], 0.0)
                    for ic in range(nch):
                        sl = slice(ic * T, (ic + 1) * T)
                        v.scalar_tensor_tensor(macc[:], src[:, sl], 1.0,
                                               macc[:], ALU.mult, ALU.max)
                        lv = v.scalar_tensor_tensor(macc[:], src[:, sl], -1.0,
                                               macc[:], ALU.mult, ALU.max)
                    s_v.inc(lv)
                if emit_rows_from_macc(macc, clip_lo):
                    emit_dump(macc, 1, r0=640)
                    return True

        def emit_quant(src, dst, nch, mult_bc):
            with contextlib.ExitStack() as qh:
                qt = [sb("qt", [128, T], F32, qh) for _ in range(2)]
                qf = [sb("qf", [128, T], F16, qh) for _ in range(2)]
                V0, A0 = s_v.v, s_a.v

                @b.vector
                def _(v):
                    for ic in range(nch):
                        sl = slice(ic * T, (ic + 1) * T)
                        if ic >= 2:
                            v.wait_ge(s_a.h, A0 + ic - 1)
                        lv = v.tensor_tensor(qt[ic % 2][:], src[:, sl], mult_bc[:],
                                        ALU.mult)
                        s_v.inc(lv)
                        v.wait_ge(s_a.h, A0 + ic + 1)
                        s_v.inc(v.tensor_scalar_add(dst[:, sl], qf[ic % 2][:],
                                                    -RND))

                @b.scalar
                def _(sc):
                    for ic in range(nch):
                        sc.wait_ge(s_v.h, V0 + 2 * ic + 1)
                        s_a.inc(sc.activation(qf[ic % 2][:], qt[ic % 2][:],
                                              AF.Copy, bias=RND))

        def emit_ffn(layer, sb_y):
            gk, vk, dk = (("gateT", "valT", "outT") if layer == 0 else
                          ("wgT", "wuT", "wdT"))
            gi, vi, di = ((0, 1, 2) if layer == 0 else (3, 4, 5))
            has_bias = layer == 0

            if emit_gamma_rows(sb_y, 16, clip_lo=(layer == 0)):
                emit_finish()
                return True
            with contextlib.ExitStack() as fh:
                xq = sb("xq", [128, 16 * T], BF16, fh)
                emit_quant(sb_y, xq, 16, sb_bc1)

                if STAGE == 41 and layer == 0:
                    emit_dump(xq, 16)
                    emit_dump(sb_bc1, 1, r0=2048)
                    emit_dump(sb_bc2, 1, r0=2048 + 128)
                    emit_finish()
                    return True

                FA = 256
                wg0 = sb("wg0", [128, 16 * FA], BF16, fh)
                wg1 = sb("wg1", [128, 16 * FA], BF16, fh)
                wv0 = sb("wv0", [128, 16 * FA], BF16, fh)
                wv1 = sb("wv1", [128, 16 * FA], BF16, fh)
                gpre = [sb("gpre", [128, TH], F32, fh) for _ in range(2)]
                gbf = [sb("gbf", [128, TH], BF16, fh) for _ in range(2)]
                vf = [sb("vf", [128, TH], F32, fh) for _ in range(2)]
                gv = [sb("gv", [128, TH], BF16, fh) for _ in range(2)]
                gacc = sb("gacc", [128, T], F32, fh)
                ps_ga = [psum("ps_ga", [128, TH], F32, fh) for _ in range(2)]
                ps_va = [psum("ps_va", [128, TH], F32, fh) for _ in range(2)]

                W0, PE0, V0, A0, ST0 = s_w.v, s_pe.v, s_v.v, s_a.v, s_g.v
                GV0, GA0 = s_v.v, s_a.v

                @b.vector
                def _(v):
                    s_v.inc(v.memset(gacc[:], 0.0))
                V0 += 1
                VQ0 = s_v.v

                @b.sync
                def _(s):
                    s.wait_ge(s_cc.h, CC_AT[gk])
                    s.wait_ge(s_cc.h, CC_AT[vk])
                    s.wait_ge(s_v.h, VQ0)
                    s.wait_ge(s_a.h, GA0)
                    s.wait_ge(s_pe.h, PE0)
                    for bi in range(32):
                        if bi >= 2:
                            s.wait_ge(s_pe.h, PE0 + 8 * (bi - 1))
                        for r in range(8):
                            for wk, buf in ((gk, wg0 if bi % 2 == 0 else wg1),
                                            (vk, wv0 if bi % 2 == 0 else wv1)):
                                s_w.dma(s.dma_start(
                                    out=buf.rearrange("p (c f) -> p c f", f=FA)
                                        [:, 2 * r:2 * r + 2, :],
                                    in_=wbf[wk].rearrange(
                                        "(rr c p) f -> rr p c f", p=128, c=2)
                                        [r, :, :, bi * FA:(bi + 1) * FA]))

                ntk = []
                @b.tensor
                def _(t):
                    tk = 0
                    for bi in range(32):
                        t.wait_ge(s_w.h, W0 + 16 * 16 * (bi + 1))
                        bg = wg0 if bi % 2 == 0 else wg1
                        bv = wv0 if bi % 2 == 0 else wv1
                        for ol in range(2):
                            for th in range(2):
                                if tk >= 2:
                                    t.wait_ge(s_v.h, VQ0 + 6 * (tk - 1))
                                for ic in range(16):
                                    mm = t.matmul(ps_ga[tk % 2][:],
                                             bg[:, ic * FA + ol * 128:ic * FA + ol * 128 + 128],
                                             xq[:, ic * T + th * TH:ic * T + (th + 1) * TH],
                                             start=(ic == 0), stop=(ic == 15))
                                s_pe.inc(mm)
                                for ic in range(16):
                                    mm = t.matmul(ps_va[tk % 2][:],
                                             bv[:, ic * FA + ol * 128:ic * FA + ol * 128 + 128],
                                             xq[:, ic * T + th * TH:ic * T + (th + 1) * TH],
                                             start=(ic == 0), stop=(ic == 15))
                                s_pe.inc(mm)
                                ntk.append((bi * 2 + ol, th, tk, s_pe.v))
                                tk += 1
                PE_A = s_pe.v

                @b.vector
                def _(v):
                    for (hc, th, tk, pev) in ntk:
                        v.wait_ge(s_pe.h, pev)
                        g_ = gpre[tk % 2]
                        vv_ = vf[tk % 2]
                        gv_ = gv[tk % 2]
                        if tk >= 2:
                            v.wait_ge(s_g.h, ST0 + 16 * (tk - 1))
                            v.wait_ge(s_a.h, A0 + (tk - 1))
                        v.tensor_tensor(g_[:], ps_ga[tk % 2][:],
                                        sb_bc2[:, th * TH:(th + 1) * TH],
                                        ALU.mult)
                        if has_bias:
                            lv = v.tensor_scalar(g_[:], g_[:], sb_gw[:, gi:gi + 1],
                                            sb_gateb[:, hc:hc + 1],
                                            ALU.mult, ALU.add)
                        else:
                            lv = v.tensor_single_scalar(g_[:], g_[:],
                                                   sb_gw[:, gi:gi + 1], ALU.mult)
                        s_v.inc(lv)
                        v.tensor_tensor(vv_[:], ps_va[tk % 2][:],
                                        sb_bc2[:, th * TH:(th + 1) * TH],
                                        ALU.mult)
                        if has_bias:
                            lv = v.tensor_scalar(vv_[:], vv_[:], sb_gw[:, vi:vi + 1],
                                            sb_valb[:, hc:hc + 1],
                                            ALU.mult, ALU.add)
                        else:
                            lv = v.tensor_single_scalar(vv_[:], vv_[:],
                                                   sb_gw[:, vi:vi + 1], ALU.mult)
                        s_v.inc(lv, 2)
                        v.wait_ge(s_a.h, A0 + tk + 1)
                        v.tensor_tensor(gv_[:], gbf[tk % 2][:], vv_[:], ALU.mult)
                        v.scalar_tensor_tensor(
                            gacc[:, th * TH:(th + 1) * TH], gv_[:], 1.0,
                            gacc[:, th * TH:(th + 1) * TH], ALU.mult, ALU.max)
                        lv = v.scalar_tensor_tensor(
                            gacc[:, th * TH:(th + 1) * TH], gv_[:], -1.0,
                            gacc[:, th * TH:(th + 1) * TH], ALU.mult, ALU.max)
                        s_v.inc(lv, 3)
                V_A = s_v.v

                @b.scalar
                def _(sc):
                    for (hc, th, tk, pev) in ntk:
                        sc.wait_ge(s_v.h, VQ0 + 6 * tk + 1)
                        s_a.inc(sc.activation(gbf[tk % 2][:], gpre[tk % 2][:],
                                              AF.Silu))

                @b.gpsimd
                def _(g):
                    for (hc, th, tk, pev) in ntk:
                        g.wait_ge(s_v.h, VQ0 + 6 * (tk + 1))
                        s_g.dma(g.dma_start(
                            out=gvb[hc, :, th * TH:(th + 1) * TH],
                            in_=gv[tk % 2][:]))
                ST_A = s_g.v

                emit_rows_from_macc(gacc, clip_lo=(layer == 0))

            with contextlib.ExitStack() as fh2:
                gvq = sb("gvq", [128, 64 * TH], BF16, fh2)
                gld = [sb("gld", [128, TH], BF16, fh2) for _ in range(2)]
                qt2 = [sb("qt2", [128, TH], F32, fh2) for _ in range(2)]
                qf2 = [sb("qf2", [128, TH], F16, fh2) for _ in range(2)]
                wd0 = sb("wd0", [128, 64 * 128], BF16, fh2)
                wd1 = sb("wd1", [128, 64 * 128], BF16, fh2)
                dt = [sb("dt", [128, TH], F32, fh2) for _ in range(2)]
                ps_d = [psum("ps_d", [128, TH], F32, fh2) for _ in range(2)]

                for tt in range(2):
                    LDB0, VB0, AB0, WB0, PEB0 = (s_ld.v, s_v.v, s_a.v,
                                                 s_w.v, s_pe.v)

                    @b.sync
                    def _(s):
                        s.wait_ge(s_v.h, VB0)
                        s.wait_ge(s_a.h, AB0)
                        s.wait_ge(s_pe.h, PEB0)
                        s.wait_ge(s_g.h, ST_A)
                        for hc in range(64):
                            if hc >= 2:
                                s.wait_ge(s_v.h, VB0 + 2 * (hc - 1))
                            s_ld.dma(s.dma_start(
                                out=gld[hc % 2][:],
                                in_=gvb[hc, :, tt * TH:(tt + 1) * TH]))

                    @b.vector
                    def _(v):
                        for hc in range(64):
                            v.wait_ge(s_ld.h, LDB0 + 16 * (hc + 1))
                            if hc >= 2:
                                v.wait_ge(s_a.h, AB0 + hc - 1)
                            lv = v.tensor_tensor(qt2[hc % 2][:], gld[hc % 2][:],
                                            sb_bc1[:, tt * TH:(tt + 1) * TH],
                                            ALU.mult)
                            s_v.inc(lv)
                            v.wait_ge(s_a.h, AB0 + hc + 1)
                            s_v.inc(v.tensor_scalar_add(
                                gvq[:, hc * TH:(hc + 1) * TH],
                                qf2[hc % 2][:], -RND))

                    @b.scalar
                    def _(sc):
                        for hc in range(64):
                            sc.wait_ge(s_v.h, VB0 + 2 * hc + 1)
                            s_a.inc(sc.activation(qf2[hc % 2][:],
                                                  qt2[hc % 2][:],
                                                  AF.Copy, bias=RND))
                    V_GVQ = s_v.v

                    if STAGE == 45 and layer == 0 and tt == 0:
                        emit_dump(gvq, 32)
                        emit_finish()
                        return True

                    @b.sync
                    def _(s):
                        s.wait_ge(s_cc.h, CC_AT[dk])
                        s.wait_ge(s_v.h, VB0)
                        for ob in range(16):
                            if ob >= 2:
                                s.wait_ge(s_pe.h, PEB0 + ob - 1)
                            buf = wd0 if ob % 2 == 0 else wd1
                            for r in range(8):
                                s_w.dma(s.dma_start(
                                    out=buf.rearrange("p (c f) -> p c f", f=128)
                                        [:, 8 * r:8 * r + 8, :],
                                    in_=wbf[dk].rearrange(
                                        "(rr c p) f -> rr p c f", p=128, c=8)
                                        [r, :, :, ob * 128:(ob + 1) * 128]))

                    @b.tensor
                    def _(t):
                        t.wait_ge(s_v.h, V_GVQ)
                        for ob in range(16):
                            t.wait_ge(s_w.h, WB0 + 16 * 8 * (ob + 1))
                            if ob >= 2:
                                t.wait_ge(s_v.h, V_GVQ + 3 * (ob - 1))
                            buf = wd0 if ob % 2 == 0 else wd1
                            for ic in range(64):
                                mm = t.matmul(ps_d[ob % 2][:],
                                         buf[:, ic * 128:(ic + 1) * 128],
                                         gvq[:, ic * TH:(ic + 1) * TH],
                                         start=(ic == 0), stop=(ic == 63))
                            s_pe.inc(mm)
                    PE_B = s_pe.v

                    @b.vector
                    def _(v):
                        for ob in range(16):
                            v.wait_ge(s_pe.h, PEB0 + ob + 1)
                            d_ = dt[ob % 2]
                            v.tensor_tensor(d_[:], ps_d[ob % 2][:],
                                            sb_bc2[:, tt * TH:(tt + 1) * TH],
                                            ALU.mult)
                            if has_bias:
                                v.tensor_scalar(d_[:], d_[:],
                                                sb_gw[:, di:di + 1],
                                                sb_outb[:, ob:ob + 1],
                                                ALU.mult, ALU.add)
                            else:
                                v.tensor_single_scalar(d_[:], d_[:],
                                                       sb_gw[:, di:di + 1],
                                                       ALU.mult)
                            sl = slice(ob * T + tt * TH, ob * T + (tt + 1) * TH)
                            lv = v.tensor_tensor(sb_y[:, sl], d_[:], sb_y[:, sl],
                                            ALU.add)
                            s_v.inc(lv, 3)
            return None

        if emit_ffn(0, sb_y):
            return nc
        emit_ln(sb_y)

        if STAGE == 5:
            emit_dump(sb_y, 16)
            emit_finish()
            return nc

        @b.sync
        def _(s):
            s.wait_ge(s_v.h, s_v.v)
            s_st.dma(s.dma_start(
                out=ybk.rearrange("(c p) t -> p c t", p=128),
                in_=sb_y.rearrange("p (c t) -> p c t", c=16)))
            s.wait_ge(s_st.h, s_st.v)
        l0_scope.close()

        with contextlib.ExitStack() as ph:
            sb_c1 = sb("c1", [128, T], F32, ph)
            sb_s1 = sb("s1", [128, T], F32, ph)
            GV_R1 = s_v.v

            @b.sync
            def _(s):
                s.wait_ge(s_v.h, GV_R1)
                s.wait_ge(s_a.h, s_a.v)
                s_ld.dma(s.dma_start(out=sb_c1[:], in_=cosd1[:, :]))
                s_ld.dma(s.dma_start(out=sb_s1[:], in_=sins1[:, :]))
            LD_C1 = s_ld.v

            ybf_l = sb("ybfl", [128, 16 * T], BF16, ph)
            qkb = sb("qkb", [128, 16 * T], BF16, ph)
            vstage = sb("vstage", [128, 16 * T], BF16, ph)
            G_YB0 = s_g.v

            @b.gpsimd
            def _(g):
                g.wait_ge(s_st.h, s_st.v)
                s_g.dma(g.dma_start(
                    out=ybf_l.rearrange("p (c t) -> p c t", c=16),
                    in_=ybk.rearrange("(c p) t -> p c t", p=128)))
            G_YBF = s_g.v

            rop1t = sb("rop1t", [128, T], F32, ph)
            rop1u = sb("rop1u", [128, T], F32, ph)
            rop1s = sb("rop1s", [128, T], BF16, ph)

            def emit_rope1_and_send(dst_dram):
                V0 = s_v.v
                ST0 = s_st.v

                @b.vector
                def _(v):
                    v.wait_ge(s_ld.h, LD_C1)
                    for h in range(H):
                        sl = slice(h * T, (h + 1) * T)
                        if h > 0:
                            v.wait_ge(s_st.h, ST0 + 16 * h)
                        v.stream_shuffle(rop1s[:], qkb[:, sl], PAIRSWAP)
                        v.tensor_tensor(rop1t[:], qkb[:, sl], sb_c1[:], ALU.mult)
                        v.tensor_tensor(rop1u[:], rop1s[:], sb_s1[:], ALU.mult)
                        s_v.inc(v.tensor_tensor(qkb[:, sl], rop1t[:],
                                                rop1u[:], ALU.add))

                @b.sync
                def _(s):
                    for h in range(H):
                        s.wait_ge(s_v.h, V0 + h + 1)
                        s_st.dma(s.dma_start(out=dst_dram[h, :, :],
                                             in_=qkb[:, h * T:(h + 1) * T]))

            for wk, dst in (("wqT", q1s), ("wkT", k1s)):
                def evac_qk(e, i, oc, th, p, w):
                    sl = slice(oc * T + th * TH, oc * T + (th + 1) * TH)
                    return e.tensor_copy(qkb[:, sl], p[:])
                with contextlib.ExitStack() as s8:
                    emit_proj(wk, ybf_l, 16, D, evac_qk, s8, cc_key=wk,
                              rhs_wait=(lambda t: t.wait_ge(s_g.h, G_YBF)))
                emit_rope1_and_send(dst)

            def evac_wv(e, i, oc, th, p, w):
                sl = slice(oc * T + th * TH, oc * T + (th + 1) * TH)
                return e.tensor_copy(vstage[:, sl], p[:])
            with contextlib.ExitStack() as s8b:
                emit_proj("wvT", ybf_l, 16, D, evac_wv, s8b, cc_key="wvT",
                          rhs_wait=(lambda t: t.wait_ge(s_g.h, G_YBF)))
            V_VST = s_v.v

            vts1 = sb("vts1", [128, 8 * 132], BF16, ph)
            ps_t1 = psum("ps_t1", [128, 8 * 128], BF16, ph)

            @b.vector
            def _(v):
                for g in range(8):
                    v.memset(vts1[:, g * 132 + 128:g * 132 + 129], 1.0)
                s_v.inc(v.memset(
                    vts1.rearrange("p (g e) -> p g e", g=8)[:, :, 129:132], 0.0))
            V_VTS1 = s_v.v

            PE_T1 = s_pe.v
            ST_T1 = s_st.v

            @b.tensor
            def _(t):
                t.wait_ge(s_v.h, V_VTS1)
                for h in range(H):
                    if h > 0:
                        t.wait_ge(s_v.h, V_VTS1 + h)
                    for g in range(8):
                        mm = t.transpose(ps_t1[:, g * 128:(g + 1) * 128],
                                    vstage[:, h * T + g * 128:h * T + (g + 1) * 128],
                                    sb_idb[:])
                    s_pe.inc(mm)

            @b.vector
            def _(v):
                for h in range(H):
                    v.wait_ge(s_pe.h, PE_T1 + h + 1)
                    if h > 0:
                        v.wait_ge(s_st.h, ST_T1 + 16 * h)
                    for g in range(8):
                        cp = v.tensor_copy(vts1[:, g * 132:g * 132 + 128],
                                      ps_t1[:, g * 128:(g + 1) * 128])
                    s_v.inc(cp)

            @b.sync
            def _(s):
                for h in range(H):
                    s.wait_ge(s_v.h, V_VTS1 + h + 1)
                    s_st.dma(s.dma_start(
                        out=v1s.rearrange("h (g p) e -> h p g e", p=128)[h],
                        in_=vts1.rearrange("p (g e) -> p g e", g=8)))

        @b.gpsimd
        def _(g):
            g.wait_ge(s_st.h, s_st.v)
            with allow_ici_all_to_all():
                for si, so in ((q1s, q1g), (k1s, k1g), (v1s, v1g)):
                    s_cc.inc(g.collective_compute(
                        "AllToAll", ALU.bypass,
                        replica_groups=[list(range(NCORES))],
                        ins=[si.opt()], outs=[so.opt()]))
        CC_A2A1 = s_cc.v

        for j in range(2):
            emit_attn(1, j, q1g, k1g, v1g, o1s, CC_A2A1)

        @b.gpsimd
        def _(g):
            g.wait_ge(s_st.h, s_st.v)
            with allow_ici_all_to_all():
                s_cc.inc(g.collective_compute(
                    "AllToAll", ALU.bypass, replica_groups=[list(range(NCORES))],
                    ins=[o1s.opt()], outs=[o1g.opt()]))
        CC_O1 = s_cc.v

        l1_scope = stack.enter_context(contextlib.ExitStack())
        sb_y2 = sb("y2s", [128, 16 * T], F32, l1_scope)
        emit_oproj_ln(1, o1g, "wo1T", CC_O1, sb_y2, load_from=ybk)

        if STAGE == 6:
            emit_dump(sb_y2, 16)
            emit_finish()
            return nc

        emit_ffn(1, sb_y2)
        emit_ln(sb_y2)
        emit_finish(ysrc=sb_y2)
        l1_scope.close()

    return nc



_PERM0 = np.empty(64, dtype=np.int64)
_PERM0[0::2] = np.arange(32)
_PERM0[1::2] = np.arange(32) + 32
_PERM1 = np.empty(128, dtype=np.int64)
_PERM1[0::2] = np.arange(64)
_PERM1[1::2] = np.arange(64) + 64


def _rope_tables0():
    inv = 1.0 / (np.float32(10000.0) **
                 (np.arange(0, RO, 2, dtype=np.float32) / np.float32(RO)))
    t = np.arange(S, dtype=np.float32)
    ang = np.outer(t, inv).astype(np.float32)
    cos = np.cos(ang).astype(np.float32)
    sin = np.sin(ang).astype(np.float32)
    cosd = np.empty((64, S), np.float32)
    sins = np.empty((64, S), np.float32)
    cosd[0::2] = cos.T
    cosd[1::2] = cos.T
    sins[0::2] = -sin.T
    sins[1::2] = sin.T
    return (np.concatenate([cosd, cosd], 0),
            np.concatenate([sins, sins], 0))


def _rope_tables1():
    theta = np.float32(10000.0) ** (
        -np.arange(64, dtype=np.float32) * np.float32(2.0) / np.float32(128.0))
    pos = np.arange(S, dtype=np.float32)
    ang = (pos[:, None] * theta[None, :]).astype(np.float32)
    cos = np.cos(ang).astype(np.float32)
    sin = np.sin(ang).astype(np.float32)
    cosd = np.empty((128, S), np.float32)
    sins = np.empty((128, S), np.float32)
    cosd[0::2] = cos.T
    cosd[1::2] = cos.T
    sins[0::2] = -sin.T
    sins[1::2] = sin.T
    return cosd, sins


def _pp(vec, nch):
    return np.ascontiguousarray(vec.reshape(nch, 128).T)


def _pad_rows(a, rows):
    if a.shape[0] == rows:
        return a
    out = np.zeros((rows, a.shape[1]), a.dtype)
    out[:a.shape[0]] = a
    return out


def kernel(L0_attn_hidden_states, L0_attn_q_a_w, L0_attn_q_a_ln_w,
           L0_attn_q_b_w, L0_attn_kv_a_w, L0_attn_kv_a_ln_w, L0_attn_kv_b_w,
           L0_attn_o_w, L0_ffn_gate_w, L0_ffn_gate_b, L0_ffn_val_w,
           L0_ffn_val_b, L0_ffn_out_w, L0_ffn_out_b, L1_attn_Wq, L1_attn_Wk,
           L1_attn_Wv, L1_attn_Wo, L1_ffn_w_gate, L1_ffn_w_up, L1_ffn_w_down):
    f32 = np.float32
    x = np.asarray(L0_attn_hidden_states, f32)[0]
    xT = np.ascontiguousarray(x.T)

    qbT = _pad_rows(np.ascontiguousarray(np.asarray(L0_attn_q_b_w, f32).T), D).copy()
    for h in range(H):
        blk = qbT[:, h * QHD + NOPE:h * QHD + QHD]
        qbT[:, h * QHD + NOPE:h * QHD + QHD] = blk[:, _PERM0]
    kvaT = np.ascontiguousarray(np.asarray(L0_attn_kv_a_w, f32).T).copy()
    kvaT[:, CKV:KVA] = kvaT[:, CKV:KVA][:, _PERM0]
    wqT = np.ascontiguousarray(np.asarray(L1_attn_Wq, f32).T).copy()
    wkT = np.ascontiguousarray(np.asarray(L1_attn_Wk, f32).T).copy()
    for h in range(H):
        wqT[:, h * 128:(h + 1) * 128] = wqT[:, h * 128:(h + 1) * 128][:, _PERM1]
        wkT[:, h * 128:(h + 1) * 128] = wkT[:, h * 128:(h + 1) * 128][:, _PERM1]

    wts = {
        "qaT": np.ascontiguousarray(np.asarray(L0_attn_q_a_w, f32).T),
        "qbT": qbT,
        "kvaT": kvaT,
        "kvbT": _pad_rows(np.ascontiguousarray(np.asarray(L0_attn_kv_b_w, f32).T), 1024),
        "o0T": np.ascontiguousarray(np.asarray(L0_attn_o_w, f32).T),
        "gateT": np.ascontiguousarray(np.asarray(L0_ffn_gate_w, f32).T),
        "valT": np.ascontiguousarray(np.asarray(L0_ffn_val_w, f32).T),
        "outT": np.ascontiguousarray(np.asarray(L0_ffn_out_w, f32).T),
        "wqT": wqT, "wkT": wkT,
        "wvT": np.ascontiguousarray(np.asarray(L1_attn_Wv, f32).T),
        "wo1T": np.ascontiguousarray(np.asarray(L1_attn_Wo, f32).T),
        "wgT": np.ascontiguousarray(np.asarray(L1_ffn_w_gate, f32).T),
        "wuT": np.ascontiguousarray(np.asarray(L1_ffn_w_up, f32).T),
        "wdT": np.ascontiguousarray(np.asarray(L1_ffn_w_down, f32).T),
    }

    cos0, sin0 = _rope_tables0()
    cos1, sin1 = _rope_tables1()
    consts = {
        "qalnw": _pp(np.asarray(L0_attn_q_a_ln_w, f32), QA // 128),
        "kvalnw": _pp(np.asarray(L0_attn_kv_a_ln_w, f32), CKV // 128),
        "gate_bi": _pp(np.asarray(L0_ffn_gate_b, f32), FF // 128),
        "val_bi": _pp(np.asarray(L0_ffn_val_b, f32), FF // 128),
        "out_bi": _pp(np.asarray(L0_ffn_out_b, f32), D // 128),
        "ident_f": np.eye(128, dtype=f32),
        "ident_b": np.eye(128, dtype=f32).astype(ml_dtypes.bfloat16),
        "ut_b": np.triu(np.ones((128, 128), f32)).astype(ml_dtypes.bfloat16),
    }

    nc = build_program()
    in_maps = []
    for c in range(NCORES):
        tok = slice(c * T, (c + 1) * T)
        im = {"xT": np.ascontiguousarray(xT[:, tok]),
              "cosd0": np.ascontiguousarray(cos0[:, tok]),
              "sins0": np.ascontiguousarray(sin0[:, tok]),
              "cosd1": np.ascontiguousarray(cos1[:, tok]),
              "sins1": np.ascontiguousarray(sin1[:, tok])}
        for k, a in wts.items():
            pr = a.shape[0] // 8
            im[k] = np.ascontiguousarray(a[c * pr:(c + 1) * pr])
        im.update(consts)
        in_maps.append(im)

    res = run_bass_kernel_spmd(nc, in_maps, core_ids=list(range(NCORES)))
    out = np.empty((1, S, D), np.float32)
    for c in range(NCORES):
        out[0, c * T:(c + 1) * T, :] = np.asarray(res.results[c]["out"]).T
    if STAGE != 99:
        dbgs = [np.asarray(res.results[c]["dbg"]) for c in range(NCORES)]
        np.save("/tmp/kmk_dbg.npy", np.stack(dbgs))
    return out


if __name__ == "__main__":
    rng = np.random.default_rng(0)
    ins = {}
    import json
    x = rng.standard_normal((1, S, D)).astype(np.float32)
    print("smoke build only")
    nc = build_program()
    print("built")
